# version 02 (final)
# baseline (speedup 1.0000x reference)
import os
import numpy as np
from contextlib import ExitStack

import concourse.bass as bass
import concourse.mybir as mybir
from concourse.bass_utils import run_bass_kernel_spmd
from concourse.alu_op_type import AluOpType as op

F32 = mybir.dt.float32
BF16 = mybir.dt.bfloat16
I32 = mybir.dt.int32
AF = mybir.ActivationFunctionType

D = 2048
KC = 16
TOK = 2048
NB = 4
S_FULL = 4096
INTER = 5504
OC = 43
DKV = 512
NQG = 4
NHA = 8
NHB = 16
KT_SCHED = [8, 16, 24, 32]
SC_A = 0.125
SC_B = 0.08838834764831845
G_LO = [0, 3, 4, 7]
G_HI = [1, 2, 5, 6]

CUT = int(os.environ.get("MK_CUT", "5"))


class Sem:
    def __init__(self, nc, es, name):
        self.h = es.enter_context(nc.semaphore(name))
        self.n = 0

    def inc(self, instr, k=1):
        instr.then_inc(self.h, k)
        self.n += k
        return self.n


def kt_loc(kt):
    g = kt // 4
    if g in G_LO:
        return 0, G_LO.index(g), kt % 4
    return 1, G_HI.index(g), kt % 4


def build_program():
    nc = bass.Bass("TRN2", target_bir_lowering=False)
    dt = nc.dram_tensor

    xT = dt("xT", [D, TOK], F32, kind="ExternalInput").ap()
    wq_in = dt("wq_blk", [KC, KC, 128, 128], F32, kind="ExternalInput").ap()
    wk_in = dt("wk_blk", [4, KC, 128, 128], F32, kind="ExternalInput").ap()
    wv_in = dt("wv_blk", [KC, 128, 512], F32, kind="ExternalInput").ap()
    mem_in = dt("mem_blk", [4, 4, 128, 128], F32, kind="ExternalInput").ap()
    mnorm_in = dt("mnorm_blk", [128, 4], F32, kind="ExternalInput").ap()
    mw_in = dt("mw", [1, 1], F32, kind="ExternalInput").ap()
    wab_in = dt("wab_blk", [20, KC, 128, 128], F32, kind="ExternalInput").ap()
    wabv_in = dt("wabv_blk", [KC, 128, 512], F32, kind="ExternalInput").ap()
    wp_in = dt("wp_blk", [KC, KC, 128, 128], F32, kind="ExternalInput").ap()
    nw_in = dt("nw", [128, 16], F32, kind="ExternalInput").ap()
    idxg_in = dt("idxg_sh", [256, INTER], I32, kind="ExternalInput").ap()
    idxu_in = dt("idxu_sh", [256, INTER], I32, kind="ExternalInput").ap()
    idxd_in = dt("idxd_sh", [768, D], I32, kind="ExternalInput").ap()
    lut_in = dt("lut", [1, 16], F32, kind="ExternalInput").ap()
    cos_in = dt("cosb", [128, TOK], F32, kind="ExternalInput").ap()
    sin_in = dt("sinb", [128, TOK], F32, kind="ExternalInput").ap()
    cmask_in = dt("cmask", [4, 8, 128, 512], F32, kind="ExternalInput").ap()
    ident_in = dt("ident", [128, 128], F32, kind="ExternalInput").ap()
    perm_in = dt("perm", [128, 128], F32, kind="ExternalInput").ap()
    outT = dt("outT", [D, TOK], F32, kind="ExternalOutput").ap()

    res = dt("res", [D, TOK], F32)
    qa_d = dt("qa_d", [D, TOK], BF16)
    am_d = dt("am_d", [D, TOK], F32)
    qb_d = dt("qb_d", [D, TOK], BF16)
    wg_sh = dt("wg_sh", [2, OC, 128, 128], BF16)
    wu_sh = dt("wu_sh", [2, OC, 128, 128], BF16)
    wd_sh = dt("wd_sh", [6, KC, 128, 128], BF16)
    wg_all = dt("wg_all", [16, OC, 128, 128], BF16)
    wu_all = dt("wu_all", [16, OC, 128, 128], BF16)
    wd_all = dt("wd_all", [48, KC, 128, 128], BF16)
    ka_in = dt("ka_in", [DKV, TOK], BF16)
    va_in = dt("va_in", [TOK, DKV], BF16)
    ka_g = dt("ka_g", [2 * DKV, TOK], BF16)
    va_g = dt("va_g", [2 * TOK, DKV], BF16)
    kb_in = dt("kb_in", [DKV, TOK], BF16)
    vb_in = dt("vb_in", [TOK, DKV], BF16)
    kb_g = dt("kb_g", [2 * DKV, TOK], BF16)
    vb_g = dt("vb_g", [2 * TOK, DKV], BF16)

    PAIRS = [[0, 1], [2, 3], [4, 5], [6, 7]]
    ALL8 = [[0, 1, 2, 3, 4, 5, 6, 7]]

    es = ExitStack()
    with es:
        sbt = lambda n_, s_, d_: es.enter_context(nc.sbuf_tensor(n_, s_, d_))
        ident = sbt("identb", [128, 128], BF16)
        identf = sbt("identf", [128, 128], F32)
        perm = sbt("permb", [128, 128], BF16)
        onesc = sbt("onesc", [128, 1], BF16)
        onesrf = sbt("onesrf", [1, 128], F32)
        nw_sb = sbt("nw_sb", [128, 16], F32)
        lutbc = sbt("lutbc", [128, 16], F32)
        lut_s = sbt("lut_s", [1, 16], F32)
        lut_bf = sbt("lut_bf", [1, 16], BF16)
        wvec = sbt("wvec", [128, 1], F32)
        c1vec = sbt("c1vec", [128, 1], F32)
        mw_s = sbt("mw_s", [1, 1], F32)
        sigw = sbt("sigw", [1, 1], F32)
        cos_sb = sbt("cos_sb", [128, TOK], F32)
        sin_sb = sbt("sin_sb", [128, TOK], F32)
        row_a = sbt("row_a", [1, 512], F32)
        row_b = sbt("row_b", [1, 512], F32)
        row_c = sbt("row_c", [1, 512], F32)
        bc_a = sbt("bc_a", [128, 512], F32)
        bc_b = sbt("bc_b", [128, 512], F32)

        pbt = [es.enter_context(nc.psum_tensor(f"pb{i}", [128, 1024], F32))
               for i in range(4)]

        sL = Sem(nc, es, "sL")
        sG = Sem(nc, es, "sG")
        sT = Sem(nc, es, "sT")
        sA = Sem(nc, es, "sA")
        sV = Sem(nc, es, "sV")
        sP = Sem(nc, es, "sP")
        sC = Sem(nc, es, "sC")

        block = es.enter_context(nc.Block())

        def barrier():
            vals = [(s, s.n) for s in (sL, sG, sT, sA, sV, sP, sC) if s.n > 0]

            def f(e):
                for s, v in vals:
                    e.wait_ge(s.h, v)
            block.sync(f)
            block.tensor(f)
            block.scalar(f)
            block.vector(f)
            block.gpsimd(f)

        def loads0(sy):
            sL.inc(sy.dma_start(out=identf[:, :], in_=ident_in), 16)
            sL.inc(sy.dma_start(out=lut_s[:, :], in_=lut_in), 16)
            sL.inc(sy.dma_start(out=mw_s[:, :], in_=mw_in), 16)
            sL.inc(sy.dma_start(out=nw_sb[:, :], in_=nw_in), 16)
        block.sync(loads0)

        def gl0(g):
            sG.inc(g.dma_start(out=perm[:, :], in_=perm_in), 16)
            sG.inc(g.dma_start(out=cos_sb[:, :], in_=cos_in), 16)
            sG.inc(g.dma_start(out=sin_sb[:, :], in_=sin_in), 16)
            sG.inc(g.dma_start(out=ident[:, :], in_=ident_in), 16)
            sP.inc(g.memset(onesc[:, :], 1.0))
            sP.inc(g.memset(onesrf[:, :], 1.0))
        block.gpsimd(gl0)

        def v0(v):
            v.wait_ge(sL.h, sL.n)
            v.wait_ge(sG.h, sG.n)
            sV.inc(v.tensor_copy(lut_bf[:, :], lut_s[:, :]))
        block.vector(v0)

        def a0(s):
            s.wait_ge(sL.h, sL.n)
            sA.inc(s.activation(sigw[:, :], mw_s[:, :], AF.Sigmoid))
        block.scalar(a0)

        def t0f(t):
            t.wait_ge(sA.h, sA.n)
            t.wait_ge(sP.h, sP.n)
            t.wait_ge(sV.h, sV.n)
            sT.inc(t.matmul(pbt[0][:, 0:1], onesrf[:, :], sigw[:, :],
                            start=True, stop=True))
            sT.inc(t.matmul(pbt[0][:, 16:32], onesrf[:, :], lut_s[:, :],
                            start=True, stop=True))
        block.tensor(t0f)

        def v1(v):
            v.wait_ge(sT.h, sT.n)
            sV.inc(v.tensor_copy(wvec[:, :], pbt[0][:, 0:1]))
            sV.inc(v.tensor_copy(lutbc[:, :], pbt[0][:, 16:32]))
            sV.inc(v.tensor_scalar(c1vec[:, :], wvec[:, :], -1.0, 1.0,
                                   op.mult, op.add))
        block.vector(v1)
        barrier()

        def decode():
            scope = ExitStack()
            with scope:
                idx_i = scope.enter_context(nc.sbuf_tensor("idx_i", [128, INTER], I32))
                idx_bf = scope.enter_context(nc.sbuf_tensor("idx_bf", [128, INTER], BF16))
                plane = scope.enter_context(nc.sbuf_tensor("plane", [128, 512], BF16))
                decb = scope.enter_context(nc.sbuf_tensor("decb", [128, 512], BF16))

                chunks = []
                for i in range(2):
                    chunks.append((idxg_in[128 * i:128 * i + 128, :], INTER, wg_sh, i))
                for i in range(2):
                    chunks.append((idxu_in[128 * i:128 * i + 128, :], INTER, wu_sh, i))
                for i in range(6):
                    chunks.append((idxd_in[128 * i:128 * i + 128, :], D, wd_sh, i))

                for ci, (src, ncols, dst, loc) in enumerate(chunks):
                    Lv0 = sL.n

                    def load(sy, src=src, ncols=ncols):
                        if sV.n:
                            sy.wait_ge(sV.h, sV.n)
                        sL.inc(sy.dma_start(out=idx_i[:, 0:ncols], in_=src), 16)
                    block.sync(load)

                    def cast(v, ncols=ncols, Lv0=Lv0):
                        v.wait_ge(sL.h, Lv0 + 16)
                        sV.inc(v.tensor_copy(idx_bf[:, 0:ncols], idx_i[:, 0:ncols]))
                    block.vector(cast)

                    ng = (ncols + 511) // 512
                    for g in range(ng):
                        w = min(512, ncols - 512 * g)
                        Vc, Tc, Ac, Gc = sV.n, sT.n, sA.n, sG.n

                        def ts16(v, g=g, w=w, Tc=Tc):
                            for j in range(16):
                                if j == 0:
                                    if Tc:
                                        v.wait_ge(sT.h, Tc)
                                else:
                                    v.wait_ge(sT.h, Tc + j)
                                sV.inc(v.tensor_scalar(
                                    plane[:, 0:w], idx_bf[:, 512 * g:512 * g + w],
                                    float(j), lutbc[:, j:j + 1], op.is_equal, op.mult))
                        block.vector(ts16)

                        def mm16(t, w=w, Vc=Vc, Ac=Ac):
                            for j in range(16):
                                t.wait_ge(sV.h, Vc + j + 1)
                                if j == 0 and Ac:
                                    t.wait_ge(sA.h, Ac)
                                sT.inc(t.matmul(pbt[0][:, 0:w], ident[:, :],
                                                plane[:, 0:w],
                                                start=(j == 0), stop=(j == 15)))
                        block.tensor(mm16)
                        Tn = sT.n

                        def evac(s, w=w, Tn=Tn, Gc=Gc):
                            s.wait_ge(sT.h, Tn)
                            if Gc:
                                s.wait_ge(sG.h, Gc)
                            sA.inc(s.activation(decb[:, 0:w], pbt[0][:, 0:w], AF.Copy))
                        block.scalar(evac)
                        An = sA.n

                        def store(g_, g2=g, w=w, dst=dst, loc=loc, An=An):
                            g_.wait_ge(sA.h, An)
                            no = w // 128
                            src2 = decb[:, 0:w].rearrange("p (o c) -> p o c", c=128)
                            d2 = dst[loc, 4 * g2:4 * g2 + no].rearrange("o p c -> p o c")
                            sG.inc(g_.dma_start(out=d2, in_=src2), 16)
                        block.gpsimd(store)

                Gv = sG.n

                def cc(g_):
                    g_.wait_ge(sG.h, Gv)
                    sC.inc(g_.collective_compute(
                        "AllGather", mybir.AluOpType.bypass, replica_groups=ALL8,
                        ins=[wg_sh.ap().opt()], outs=[wg_all.ap().opt()]))
                    sC.inc(g_.collective_compute(
                        "AllGather", mybir.AluOpType.bypass, replica_groups=ALL8,
                        ins=[wu_sh.ap().opt()], outs=[wu_all.ap().opt()]))
                    sC.inc(g_.collective_compute(
                        "AllGather", mybir.AluOpType.bypass, replica_groups=ALL8,
                        ins=[wd_sh.ap().opt()], outs=[wd_all.ap().opt()]))
                block.gpsimd(cc)

        decode()
        barrier()

        ln_uid = [0]

        def emit_ln(x_src, xn_dst, tb_list, width):
            ln_uid[0] += 1
            u = ln_uid[0]
            scope = ExitStack()
            with scope:
                xcb = [scope.enter_context(
                    nc.sbuf_tensor(f"ln_xc{u}{i}", [128, 512], F32)) for i in range(2)]
                xbf = scope.enter_context(nc.sbuf_tensor(f"ln_xbf{u}", [128, 512], BF16))
                xsq = scope.enter_context(nc.sbuf_tensor(f"ln_xsq{u}", [128, 512], BF16))
                for (co_src, co_dst) in tb_list:
                    use1 = {}
                    use1v = {}
                    Lbase = sL.n

                    def load1(sy, co_src=co_src):
                        for c in range(KC):
                            if c >= 2:
                                sy.wait_ge(sV.h, use1v[c - 2])
                                sy.wait_ge(sA.h, use1[c - 2])
                            elif sV.n:
                                sy.wait_ge(sV.h, sV.n)
                                sy.wait_ge(sA.h, sA.n)
                            sL.inc(sy.dma_start(
                                out=xcb[c % 2][:, :],
                                in_=x_src[128 * c:128 * c + 128, co_src:co_src + 512]), 16)
                    for c in range(KC):
                        Lv0 = sL.n
                        u1 = use1.get(c - 2, 0)
                        u1v = use1v.get(c - 2, 0)
                        Vp, Ap = sV.n, sA.n

                        def load(sy, c=c, co_src=co_src, u1=u1, u1v=u1v, Vp=Vp, Ap=Ap):
                            if c >= 2:
                                sy.wait_ge(sV.h, u1v)
                                sy.wait_ge(sA.h, u1)
                            elif c == 0 and Vp:
                                sy.wait_ge(sV.h, Vp)
                                sy.wait_ge(sA.h, Ap)
                            sL.inc(sy.dma_start(
                                out=xcb[c % 2][:, :],
                                in_=x_src[128 * c:128 * c + 128, co_src:co_src + 512]), 16)
                        block.sync(load)
                        Lv = sL.n
                        Vc, Ac, Tc = sV.n, sA.n, sT.n

                        def cst(v, c=c, Lv=Lv, Tc=Tc):
                            v.wait_ge(sL.h, Lv)
                            if c > 0:
                                v.wait_ge(sT.h, Tc)
                            sV.inc(v.tensor_copy(xbf[:, :], xcb[c % 2][:, :]))
                        block.vector(cst)
                        use1v[c] = sV.n

                        def sq(s_, c=c, Lv=Lv, Tc=Tc):
                            s_.wait_ge(sL.h, Lv)
                            if c > 0:
                                s_.wait_ge(sT.h, Tc)
                            sA.inc(s_.activation(xsq[:, :], xcb[c % 2][:, :], AF.Square))
                        block.scalar(sq)
                        use1[c] = sA.n

                        def mm(t, c=c, Vc=Vc, Ac=Ac):
                            t.wait_ge(sV.h, Vc + 1)
                            t.wait_ge(sA.h, Ac + 1)
                            sT.inc(t.matmul(pbt[2][0:1, 0:512], onesc[:, :], xbf[:, :],
                                            start=(c == 0), stop=(c == KC - 1)))
                            sT.inc(t.matmul(pbt[2][0:1, 512:1024], onesc[:, :], xsq[:, :],
                                            start=(c == 0), stop=(c == KC - 1)))
                        block.tensor(mm)

                    Tv = sT.n

                    def rows(v, Tv=Tv):
                        v.wait_ge(sT.h, Tv)
                        sV.inc(v.tensor_scalar(row_a[:, :], pbt[2][0:1, 0:512],
                                               1.0 / D, None, op.mult))
                        sV.inc(v.tensor_scalar(row_b[:, :], pbt[2][0:1, 512:1024],
                                               1.0 / D, None, op.mult))
                        sV.inc(v.tensor_tensor(row_c[:, :], row_a[:, :], row_a[:, :],
                                               op.mult))
                        sV.inc(v.tensor_tensor(row_b[:, :], row_b[:, :], row_c[:, :],
                                               op.subtract))
                        sV.inc(v.tensor_scalar(row_b[:, :], row_b[:, :], 1e-5, None,
                                               op.add))
                    block.vector(rows)
                    Vv = sV.n

                    def rstd(s_, Vv=Vv):
                        s_.wait_ge(sV.h, Vv)
                        sA.inc(s_.activation(row_c[:, :], row_b[:, :], AF.Ln))
                        sA.inc(s_.activation(row_b[:, :], row_c[:, :], AF.Exp, scale=-0.5))
                    block.scalar(rstd)
                    Av = sA.n

                    def bc(t, Av=Av):
                        t.wait_ge(sA.h, Av)
                        sT.inc(t.matmul(pbt[3][:, 0:512], onesrf[:, :], row_a[:, :],
                                        start=True, stop=True))
                        sT.inc(t.matmul(pbt[3][:, 512:1024], onesrf[:, :], row_b[:, :],
                                        start=True, stop=True))
                    block.tensor(bc)
                    Tv2 = sT.n

                    def evbc(s_, Tv2=Tv2):
                        s_.wait_ge(sT.h, Tv2)
                        sA.inc(s_.activation(bc_a[:, :], pbt[3][:, 0:512], AF.Copy))
                        sA.inc(s_.activation(bc_b[:, :], pbt[3][:, 512:1024], AF.Copy))
                    block.scalar(evbc)
                    Av2 = sA.n

                    use2 = {}
                    for c in range(KC):
                        u2 = use2.get(c - 2, 0)

                        def load2(sy, c=c, co_src=co_src, u2=u2, Av2=Av2):
                            if c >= 2:
                                sy.wait_ge(sV.h, u2)
                            elif c == 0:
                                sy.wait_ge(sA.h, Av2)
                            sL.inc(sy.dma_start(
                                out=xcb[c % 2][:, :],
                                in_=x_src[128 * c:128 * c + 128, co_src:co_src + 512]), 16)
                        block.sync(load2)
                        Lv = sL.n

                        def norm(v, c=c, co_dst=co_dst, Lv=Lv, Av2=Av2):
                            v.wait_ge(sL.h, Lv)
                            v.wait_ge(sA.h, Av2)
                            sV.inc(v.tensor_tensor(xcb[c % 2][:, :], xcb[c % 2][:, :],
                                                   bc_a[:, :], op.subtract))
                            sV.inc(v.tensor_tensor(
                                xn_dst[:, width * c + co_dst:width * c + co_dst + 512],
                                xcb[c % 2][:, :], bc_b[:, :], op.mult))
                        block.vector(norm)
                        use2[c] = sV.n

        def phase_a1():
            vq_marks = {}
            elu_marks = {}
            qa_store_marks = {}
            am_marks = {}
            q_t_marks = {}
            guard = [0]
            scope = ExitStack()
            with scope:
                xn = scope.enter_context(nc.sbuf_tensor("a1_xn", [128, KC * 2048], BF16))
                emit_ln(xT, xn, [(512 * i, 512 * i) for i in range(NB)], 2048)
                barrier()

                wq_sb = [scope.enter_context(nc.sbuf_tensor(f"wq{i}", [128, KC * 128], BF16))
                         for i in range(2)]
                sig = scope.enter_context(nc.sbuf_tensor("sig", [128, 4 * 2048], BF16))
                qa_bf = scope.enter_context(nc.sbuf_tensor("qa_bf", [128, 512], BF16))
                ea = scope.enter_context(nc.sbuf_tensor("ea", [128, 512], BF16))
                eb = scope.enter_context(nc.sbuf_tensor("eb", [128, 512], BF16))
                ec = scope.enter_context(nc.sbuf_tensor("ec", [128, 512], BF16))
                am_bf = scope.enter_context(nc.sbuf_tensor("am_bf", [128, 512], F32))
                mem_sb = scope.enter_context(nc.sbuf_tensor("mem_sb", [128, 16 * 128], BF16))
                mn_sb = scope.enter_context(nc.sbuf_tensor("mn_sb", [128, 4], BF16))
                dbc = scope.enter_context(nc.sbuf_tensor("dbc", [128, 512], F32))

                def memload(g):
                    for i in range(4):
                        src = mem_in[i].rearrange("j p c -> p j c")
                        dstm = mem_sb[:, 512 * i:512 * i + 512].rearrange(
                            "p (j c) -> p j c", c=128)
                        sG.inc(g.dma_start(out=dstm, in_=src), 16)
                    sG.inc(g.dma_start(out=mn_sb[:, :], in_=mnorm_in), 16)
                block.gpsimd(memload)
                GM = sG.n

                T0 = sT.n
                for mc in range(KC):
                    qg, hpair = mc // 4, mc % 4
                    qtm = q_t_marks.get(mc - 2, 0)

                    def wql(g, mc=mc, qtm=qtm):
                        if qtm:
                            g.wait_ge(sT.h, qtm)
                        src = wq_in[mc].rearrange("k p c -> p k c")
                        sG.inc(g.dma_start(
                            out=wq_sb[mc % 2][:, :].rearrange("p (k c) -> p k c", c=128),
                            in_=src), 16)
                    block.gpsimd(wql)
                    GW = sG.n

                    for tb in range(NB):
                        it = mc * NB + tb
                        hb = it % 4
                        ph = pbt[hb // 2][:, 512 * (hb % 2):512 * (hb % 2) + 512]

                        def mmq(t, mc=mc, tb=tb, it=it, ph=ph, GW=GW):
                            t.wait_ge(sG.h, GW)
                            if it >= 4:
                                t.wait_ge(sV.h, max(vq_marks[it - 4], guard[0]))
                            for kc in range(KC):
                                sT.inc(t.matmul(
                                    ph, wq_sb[mc % 2][:, 128 * kc:128 * kc + 128],
                                    xn[:, 2048 * kc + 512 * tb:2048 * kc + 512 * tb + 512],
                                    start=(kc == 0), stop=(kc == KC - 1)))
                        block.tensor(mmq)
                        Tv = sT.n

                        def ev(v, Tv=Tv, it=it, ph=ph):
                            v.wait_ge(sT.h, Tv)
                            if it >= 2:
                                v.wait_ge(sG.h, qa_store_marks[it - 2])
                            if it >= 1:
                                v.wait_ge(sA.h, elu_marks[it - 1])
                            sV.inc(v.tensor_copy(qa_bf[:, :], ph))
                        block.vector(ev)
                        vq_marks[it] = sV.n
                        Vv = sV.n

                        def stq(g, mc=mc, tb=tb, Vv=Vv):
                            g.wait_ge(sV.h, Vv)
                            sG.inc(g.dma_start(
                                out=qa_d[128 * mc:128 * mc + 128, 512 * tb:512 * tb + 512],
                                in_=qa_bf[:, :]), 16)
                        block.gpsimd(stq)
                        qa_store_marks[it] = sG.n

                        def elu(s, Vv=Vv):
                            s.wait_ge(sV.h, Vv)
                            sA.inc(s.activation(ea[:, :], qa_bf[:, :], AF.Relu))
                            sA.inc(s.activation(eb[:, :], qa_bf[:, :], AF.Relu, scale=-1.0))
                            sA.inc(s.activation(ec[:, :], eb[:, :], AF.Exp, scale=-1.0))
                        block.scalar(elu)
                        elu_marks[it] = sA.n
                        q_t_marks[mc] = sT.n
                        Av = sA.n

                        def sgw(v, mc=mc, tb=tb, Av=Av, Tv=Tv):
                            v.wait_ge(sA.h, Av)
                            v.wait_ge(sT.h, Tv)
                            sV.inc(v.scalar_tensor_tensor(
                                sig[:, 2048 * (mc % 4) + 512 * tb:2048 * (mc % 4) + 512 * tb + 512],
                                ea[:, :], -1.0, ec[:, :], op.add, op.add))
                        block.vector(sgw)

                    if hpair == 3:
                        for tb in range(NB):
                            V0 = sV.n

                            def mmden(t, tb=tb, V0=V0):
                                t.wait_ge(sV.h, V0)
                                t.wait_ge(sG.h, GM)
                                for i in range(4):
                                    sT.inc(t.matmul(
                                        pbt[2][0:1, 0:512], mn_sb[:, i:i + 1],
                                        sig[:, 2048 * i + 512 * tb:2048 * i + 512 * tb + 512],
                                        start=(i == 0), stop=(i == 3)))
                            block.tensor(mmden)
                            Tn = sT.n

                            def rcp(v, Tn=Tn):
                                v.wait_ge(sT.h, Tn)
                                sV.inc(v.reciprocal(row_a[:, :], pbt[2][0:1, 0:512]))
                            block.vector(rcp)
                            Vn = sV.n

                            def bcd(t, Vn=Vn):
                                t.wait_ge(sV.h, Vn)
                                sT.inc(t.matmul(pbt[3][:, 0:512], onesrf[:, :],
                                                row_a[:, :], start=True, stop=True))
                            block.tensor(bcd)
                            Tn2 = sT.n

                            def evd(s, Tn2=Tn2):
                                s.wait_ge(sT.h, Tn2)
                                sA.inc(s.activation(dbc[:, :], pbt[3][:, 0:512], AF.Copy))
                            block.scalar(evd)
                            Avd = sA.n

                            for mc2 in range(4):
                                def mmam(t, tb=tb, mc2=mc2, Avd=Avd):
                                    t.wait_ge(sA.h, Avd)
                                    if mc2 >= 2:
                                        t.wait_ge(sV.h, am_marks[mc2 - 2])
                                    ph2 = pbt[mc2 % 2][:, 0:512]
                                    for i in range(4):
                                        sT.inc(t.matmul(
                                            ph2,
                                            mem_sb[:, 512 * i + 128 * mc2:512 * i + 128 * mc2 + 128],
                                            sig[:, 2048 * i + 512 * tb:2048 * i + 512 * tb + 512],
                                            start=(i == 0), stop=(i == 3)))
                                block.tensor(mmam)
                                Tn3 = sT.n
                                Gpre = sG.n

                                def amn(v, mc2=mc2, Tn3=Tn3, Gpre=Gpre):
                                    v.wait_ge(sT.h, Tn3)
                                    v.wait_ge(sG.h, Gpre)
                                    sV.inc(v.tensor_tensor(
                                        am_bf[:, :], pbt[mc2 % 2][:, 0:512],
                                        dbc[:, :], op.mult))
                                block.vector(amn)
                                am_marks[mc2] = sV.n
                                Vn3 = sV.n

                                def amst(g, qg=qg, tb=tb, mc2=mc2, Vn3=Vn3):
                                    g.wait_ge(sV.h, Vn3)
                                    r0 = 512 * qg + 128 * mc2
                                    sG.inc(g.dma_start(
                                        out=am_d[r0:r0 + 128, 512 * tb:512 * tb + 512],
                                        in_=am_bf[:, :]), 16)
                                block.gpsimd(amst)
                        guard[0] = sV.n

                wk_sb = scope.enter_context(nc.sbuf_tensor("wk_sb", [128, KC * 128], BF16))
                kv_bf = scope.enter_context(nc.sbuf_tensor("kv_bf", [128, 512], BF16))
                for mc in range(4):
                    def wkl(g, mc=mc):
                        if sT.n:
                            g.wait_ge(sT.h, sT.n)
                        src = wk_in[mc].rearrange("k p c -> p k c")
                        sG.inc(g.dma_start(
                            out=wk_sb[:, :].rearrange("p (k c) -> p k c", c=128),
                            in_=src), 16)
                    block.gpsimd(wkl)
                    G1 = sG.n
                    for tb in range(NB):
                        Vpre = sV.n

                        def mmk(t, tb=tb, G1=G1, Vpre=Vpre):
                            t.wait_ge(sG.h, G1)
                            if Vpre:
                                t.wait_ge(sV.h, Vpre)
                            for kc in range(KC):
                                sT.inc(t.matmul(
                                    pbt[0][:, 0:512],
                                    wk_sb[:, 128 * kc:128 * kc + 128],
                                    xn[:, 2048 * kc + 512 * tb:2048 * kc + 512 * tb + 512],
                                    start=(kc == 0), stop=(kc == KC - 1)))
                        block.tensor(mmk)
                        Tv = sT.n
                        Gpre = sG.n

                        def evk(v, Tv=Tv, Gpre=Gpre):
                            v.wait_ge(sT.h, Tv)
                            v.wait_ge(sG.h, Gpre)
                            sV.inc(v.tensor_copy(kv_bf[:, :], pbt[0][:, 0:512]))
                        block.vector(evk)
                        Vv = sV.n

                        def stk(g, mc=mc, tb=tb, Vv=Vv):
                            g.wait_ge(sV.h, Vv)
                            sG.inc(g.dma_start(
                                out=ka_in[128 * mc:128 * mc + 128, 512 * tb:512 * tb + 512],
                                in_=kv_bf[:, :]), 16)
                        block.gpsimd(stk)

                wv_sb = scope.enter_context(nc.sbuf_tensor("wv_sb", [128, KC * 512], BF16))

                def wvl(g):
                    if sT.n:
                        g.wait_ge(sT.h, sT.n)
                    sG.inc(g.dma_start(
                        out=wv_sb[:, :].rearrange("p (k c) -> p k c", c=512),
                        in_=wv_in.rearrange("k p c -> p k c")), 16)
                block.gpsimd(wvl)
                G2 = sG.n
                for ts_ in range(16):
                    Vpre = sV.n

                    def mmv(t, ts_=ts_, G2=G2, Vpre=Vpre):
                        t.wait_ge(sG.h, G2)
                        if Vpre:
                            t.wait_ge(sV.h, Vpre)
                        for kc in range(KC):
                            sT.inc(t.matmul(
                                pbt[0][:, 0:512],
                                xn[:, 2048 * kc + 128 * ts_:2048 * kc + 128 * ts_ + 128],
                                wv_sb[:, 512 * kc:512 * kc + 512],
                                start=(kc == 0), stop=(kc == KC - 1)))
                    block.tensor(mmv)
                    Tv = sT.n
                    Gpre = sG.n

                    def evv(v, Tv=Tv, Gpre=Gpre):
                        v.wait_ge(sT.h, Tv)
                        v.wait_ge(sG.h, Gpre)
                        sV.inc(v.tensor_copy(kv_bf[:, :], pbt[0][:, 0:512]))
                    block.vector(evv)
                    Vv = sV.n

                    def stv(g, ts_=ts_, Vv=Vv):
                        g.wait_ge(sV.h, Vv)
                        sG.inc(g.dma_start(
                            out=va_in[128 * ts_:128 * ts_ + 128, :], in_=kv_bf[:, :]), 16)
                    block.gpsimd(stv)

                Gv = sG.n

                def cca(g):
                    g.wait_ge(sG.h, Gv)
                    sC.inc(g.collective_compute(
                        "AllGather", mybir.AluOpType.bypass, replica_groups=PAIRS,
                        ins=[ka_in.ap().opt()], outs=[ka_g.ap().opt()]))
                    sC.inc(g.collective_compute(
                        "AllGather", mybir.AluOpType.bypass, replica_groups=PAIRS,
                        ins=[va_in.ap().opt()], outs=[va_g.ap().opt()]))
                block.gpsimd(cca)

        def phase_a2():
            exp_done = {0: 0, 1: 0}
            pv_done = {0: 0, 1: 0}
            po_free = {0: 0, 1: 0}
            pbc_free = {0: 0, 1: 0}
            res_marks = {}
            scope = ExitStack()
            with scope:
                kaT = scope.enter_context(nc.sbuf_tensor("kaT", [128, 8 * 2048], BF16))
                vaug = scope.enter_context(nc.sbuf_tensor("vaug", [128, 32 * 520], BF16))
                qT = scope.enter_context(nc.sbuf_tensor("qTa", [128, 16 * 512], BF16))
                amT = scope.enter_context(nc.sbuf_tensor("amT", [128, 16 * 512], F32))
                xbl = scope.enter_context(nc.sbuf_tensor("a2_x", [128, 16 * 512], F32))
                Ebuf = [scope.enter_context(nc.sbuf_tensor(f"Ea{i}", [128, 1024], BF16))
                        for i in range(2)]
                adn = scope.enter_context(nc.sbuf_tensor("adn", [128, 512], F32))
                cmb = scope.enter_context(nc.sbuf_tensor("cmb", [128, 512], F32))

                C0 = sC.n

                def ldkv(g):
                    g.wait_ge(sC.h, C0)
                    for ch in range(8):
                        sG.inc(g.dma_start(out=kaT[:, 2048 * ch:2048 * ch + 2048],
                                           in_=ka_g[128 * ch:128 * ch + 128, :]), 16)
                    for vc in range(32):
                        dstv = vaug[:, 520 * vc:520 * vc + 520].rearrange(
                            "p (h c) -> p h c", c=65)[:, :, 0:64]
                        srcv = va_g[128 * vc:128 * vc + 128, :].rearrange(
                            "p (h c) -> p h c", c=64)
                        sG.inc(g.dma_start(out=dstv, in_=srcv), 16)
                    ones_cols = vaug[:, :].rearrange(
                        "p (v c) -> p v c", c=65)[:, :, 64:65]
                    sP.inc(g.memset(ones_cols, 1.0))
                block.gpsimd(ldkv)
                GKV = sG.n
                PKV = sP.n

                for qb in range(NB):
                    Tpre, Vpre = sT.n, sV.n

                    def ldq(sy, qb=qb, Tpre=Tpre, Vpre=Vpre):
                        if Tpre:
                            sy.wait_ge(sT.h, Tpre)
                        if Vpre:
                            sy.wait_ge(sV.h, Vpre)
                        for c in range(KC):
                            sL.inc(sy.dma_start(
                                out=qT[:, 512 * c:512 * c + 512],
                                in_=qa_d[128 * c:128 * c + 128, 512 * qb:512 * qb + 512]), 16)
                        for c in range(KC):
                            sL.inc(sy.dma_start(
                                out=amT[:, 512 * c:512 * c + 512],
                                in_=am_d[128 * c:128 * c + 128, 512 * qb:512 * qb + 512]), 16)
                        for c in range(KC):
                            sL.inc(sy.dma_start(
                                out=xbl[:, 512 * c:512 * c + 512],
                                in_=xT[128 * c:128 * c + 128, 512 * qb:512 * qb + 512]), 16)
                    block.sync(ldq)
                    LQ = sL.n

                    for qg in range(NQG):
                        for h in range(NHA):
                            hc = 4 * qg + h // 2
                            b64 = 64 * (h % 2)
                            po = pbt[2 + (h % 2)][:, 0:512]
                            pbc = pbt[2 + (h % 2)][:, 512:1024]

                            for g2 in range(16):
                                ps = pbt[g2 % 2]
                                ed = exp_done[g2 % 2]

                                def mqk(t, qg=qg, h=h, g2=g2, hc=hc, b64=b64,
                                        ps=ps, ed=ed, LQ=LQ):
                                    t.wait_ge(sL.h, LQ)
                                    t.wait_ge(sG.h, GKV)
                                    t.wait_ge(sP.h, PKV)
                                    if ed:
                                        t.wait_ge(sA.h, ed)
                                    for half in range(2):
                                        kt = 2 * g2 + half
                                        p, slot, sub = kt_loc(kt)
                                        ch = 4 * p + h // 2
                                        col = 512 * slot + 128 * sub
                                        sT.inc(t.matmul(
                                            ps[:, 512 * half:512 * half + 512],
                                            kaT[b64:b64 + 64,
                                                2048 * ch + col:2048 * ch + col + 128],
                                            qT[b64:b64 + 64, 512 * hc:512 * hc + 512],
                                            start=True, stop=True))
                                block.tensor(mqk)
                                Tqk = sT.n
                                pvd = pv_done[g2 % 2]

                                def ex(s, g2=g2, ps=ps, Tqk=Tqk, pvd=pvd):
                                    s.wait_ge(sT.h, Tqk)
                                    if pvd:
                                        s.wait_ge(sT.h, pvd)
                                    sA.inc(s.activation(Ebuf[g2 % 2][:, :], ps[:, :],
                                                        AF.Exp, scale=SC_A))
                                block.scalar(ex)
                                exp_done[g2 % 2] = sA.n
                                Aex = sA.n
                                pof = po_free[h % 2]

                                def mpv(t, h=h, g2=g2, po=po, Aex=Aex, pof=pof):
                                    t.wait_ge(sA.h, Aex)
                                    if g2 == 0 and pof:
                                        t.wait_ge(sV.h, pof)
                                    for half in range(2):
                                        kt = 2 * g2 + half
                                        p, slot, sub = kt_loc(kt)
                                        vc = 16 * p + 4 * slot + sub
                                        sT.inc(t.matmul(
                                            po[0:65, :],
                                            vaug[:, 520 * vc + 65 * h:520 * vc + 65 * h + 65],
                                            Ebuf[g2 % 2][:, 512 * half:512 * half + 512],
                                            start=(kt == 0), stop=(kt == 31)))
                                block.tensor(mpv)
                                pv_done[g2 % 2] = sT.n

                            Tpv = sT.n

                            def rc(v, po=po, Tpv=Tpv):
                                v.wait_ge(sT.h, Tpv)
                                sV.inc(v.reciprocal(row_a[:, :], po[64:65, :]))
                            block.vector(rc)
                            Vr = sV.n
                            pcf = pbc_free[h % 2]

                            def bcm(t, pbc=pbc, Vr=Vr, pcf=pcf):
                                t.wait_ge(sV.h, Vr)
                                if pcf:
                                    t.wait_ge(sA.h, pcf)
                                sT.inc(t.matmul(pbc[0:64, :], onesrf[:, 0:64],
                                                row_a[:, :], start=True, stop=True))
                            block.tensor(bcm)
                            Tb = sT.n

                            def evb(s, pbc=pbc, Tb=Tb):
                                s.wait_ge(sT.h, Tb)
                                sA.inc(s.activation(bc_a[0:64, :], pbc[0:64, :], AF.Copy))
                            block.scalar(evb)
                            pbc_free[h % 2] = sA.n
                            Ab = sA.n

                            def nm(v, po=po, b64=b64, Ab=Ab):
                                v.wait_ge(sA.h, Ab)
                                sV.inc(v.tensor_tensor(adn[b64:b64 + 64, :], po[0:64, :],
                                                       bc_a[0:64, :], op.mult))
                            block.vector(nm)
                            po_free[h % 2] = sV.n

                            if h % 2 == 1:
                                Vn = sV.n
                                rm = res_marks.get((qb, hc - 1), 0)

                                def comb(v, hc=hc, Vn=Vn, rm=rm):
                                    v.wait_ge(sV.h, Vn)
                                    if rm:
                                        v.wait_ge(sG.h, rm)
                                    sV.inc(v.tensor_scalar(cmb[:, :], adn[:, :],
                                                           c1vec[:, 0:1], None, op.mult))
                                    sV.inc(v.scalar_tensor_tensor(
                                        cmb[:, :], amT[:, 512 * hc:512 * hc + 512],
                                        wvec[:, 0:1], cmb[:, :], op.mult, op.add))
                                    sV.inc(v.tensor_tensor(
                                        cmb[:, :], cmb[:, :],
                                        xbl[:, 512 * hc:512 * hc + 512], op.add))
                                block.vector(comb)
                                Vc2 = sV.n

                                def resst(g, qb=qb, hc=hc, Vc2=Vc2):
                                    g.wait_ge(sV.h, Vc2)
                                    sG.inc(g.dma_start(
                                        out=res[128 * hc:128 * hc + 128,
                                                512 * qb:512 * qb + 512],
                                        in_=cmb[:, :]), 16)
                                block.gpsimd(resst)
                                res_marks[(qb, hc)] = sG.n

        def phase_ffn(x_src, x_dst, tag):
            for sup in range(2):
                h_marks = {}
                sg_marks = {}
                d_marks = {}
                o_marks = {}
                scope = ExitStack()
                with scope:
                    xn = scope.enter_context(
                        nc.sbuf_tensor(f"f_xn{tag}{sup}", [128, KC * 1024], BF16))
                    hT = scope.enter_context(
                        nc.sbuf_tensor(f"f_h{tag}{sup}", [128, OC * 1024], BF16))
                    wg_b = [scope.enter_context(
                        nc.sbuf_tensor(f"f_wg{tag}{sup}{i}", [128, KC * 128], BF16))
                        for i in range(2)]
                    wu_b = [scope.enter_context(
                        nc.sbuf_tensor(f"f_wu{tag}{sup}{i}", [128, KC * 128], BF16))
                        for i in range(2)]
                    sg_b = scope.enter_context(
                        nc.sbuf_tensor(f"f_sg{tag}{sup}", [128, 512], BF16))
                    xc_b = [scope.enter_context(
                        nc.sbuf_tensor(f"f_xc{tag}{sup}{i}", [128, 1024], F32))
                        for i in range(2)]
                    oc_b = [scope.enter_context(
                        nc.sbuf_tensor(f"f_oc{tag}{sup}{i}", [128, 512], F32))
                        for i in range(2)]

                    co0 = 1024 * sup
                    emit_ln(x_src, xn, [(co0, 0), (co0 + 512, 512)], 1024)
                    barrier()

                    T0 = sT.n

                    def wload(sy):
                        for ocx in range(OC):
                            if ocx >= 2:
                                sy.wait_ge(sT.h, T0 + 64 * (ocx - 1))
                            sy.wait_ge(sC.h, 3)
                            sL.inc(sy.dma_start(
                                out=wg_b[ocx % 2][:, :].rearrange("p (k c) -> p k c", c=128),
                                in_=wg_all[:, ocx].rearrange("k p c -> p k c")), 16)
                            sL.inc(sy.dma_start(
                                out=wu_b[ocx % 2][:, :].rearrange("p (k c) -> p k c", c=128),
                                in_=wu_all[:, ocx].rearrange("k p c -> p k c")), 16)
                    block.sync(wload)
                    L0 = sL.n - 32 * OC

                    for ocx in range(OC):
                        for half in range(2):
                            it = 2 * ocx + half
                            pg = pbt[0][:, 512 * (it % 2):512 * (it % 2) + 512]
                            pu = pbt[1][:, 512 * (it % 2):512 * (it % 2) + 512]
                            hm = h_marks.get(it - 2, 0)

                            def mgu(t, ocx=ocx, half=half, pg=pg, pu=pu, hm=hm):
                                t.wait_ge(sL.h, L0 + 32 * (ocx + 1))
                                if hm:
                                    t.wait_ge(sV.h, hm)
                                for kc in range(KC):
                                    sT.inc(t.matmul(
                                        pg, wg_b[ocx % 2][:, 128 * kc:128 * kc + 128],
                                        xn[:, 1024 * kc + 512 * half:1024 * kc + 512 * half + 512],
                                        start=(kc == 0), stop=(kc == KC - 1)))
                                for kc in range(KC):
                                    sT.inc(t.matmul(
                                        pu, wu_b[ocx % 2][:, 128 * kc:128 * kc + 128],
                                        xn[:, 1024 * kc + 512 * half:1024 * kc + 512 * half + 512],
                                        start=(kc == 0), stop=(kc == KC - 1)))
                            block.tensor(mgu)
                            Tg = sT.n
                            sm = sg_marks.get(it - 1, 0)

                            def silu(s, pg=pg, Tg=Tg, sm=sm):
                                s.wait_ge(sT.h, Tg)
                                if sm:
                                    s.wait_ge(sV.h, sm)
                                sA.inc(s.activation(sg_b[:, :], pg, AF.Silu))
                            block.scalar(silu)
                            As = sA.n

                            def hmul(v, ocx=ocx, half=half, pu=pu, As=As):
                                v.wait_ge(sA.h, As)
                                sV.inc(v.tensor_tensor(
                                    hT[:, 1024 * ocx + 512 * half:1024 * ocx + 512 * half + 512],
                                    sg_b[:, :], pu, op.mult))
                            block.vector(hmul)
                            h_marks[it] = sV.n
                            sg_marks[it] = sV.n

                    HV = sV.n

                    wd_b = [scope.enter_context(
                        nc.sbuf_tensor(f"f_wd{tag}{sup}{i}", [128, OC * 128], BF16))
                        for i in range(2)]
                    T1 = sT.n

                    def wdload(sy):
                        for dhc in range(KC):
                            if dhc >= 2:
                                sy.wait_ge(sT.h, T1 + 86 * (dhc - 1))
                            sL.inc(sy.dma_start(
                                out=wd_b[dhc % 2][:, :].rearrange("p (k c) -> p k c", c=128),
                                in_=wd_all[0:OC, dhc].rearrange("k p c -> p k c")), 16)
                    block.sync(wdload)
                    L1 = sL.n - 16 * KC

                    for dhc in range(KC):
                        xm = d_marks.get(2 * dhc - 3, 0)

                        def lx(g, dhc=dhc, xm=xm):
                            if xm:
                                g.wait_ge(sV.h, xm)
                            sG.inc(g.dma_start(
                                out=xc_b[dhc % 2][:, :],
                                in_=x_src[128 * dhc:128 * dhc + 128, co0:co0 + 1024]), 16)
                        block.gpsimd(lx)
                        GX = sG.n

                        for half in range(2):
                            it = 2 * dhc + half
                            pd = pbt[2][:, 512 * (it % 2):512 * (it % 2) + 512]
                            dm = d_marks.get(it - 2, 0)

                            def mdown(t, dhc=dhc, half=half, pd=pd, dm=dm):
                                t.wait_ge(sL.h, L1 + 16 * (dhc + 1))
                                t.wait_ge(sV.h, HV)
                                if dm:
                                    t.wait_ge(sV.h, dm)
                                for ocx in range(OC):
                                    sT.inc(t.matmul(
                                        pd, wd_b[dhc % 2][:, 128 * ocx:128 * ocx + 128],
                                        hT[:, 1024 * ocx + 512 * half:1024 * ocx + 512 * half + 512],
                                        start=(ocx == 0), stop=(ocx == OC - 1)))
                            block.tensor(mdown)
                            Td = sT.n
                            om = o_marks.get(it - 2, 0)

                            def radd(v, dhc=dhc, half=half, it=it, pd=pd, Td=Td,
                                     GX=GX, om=om):
                                v.wait_ge(sT.h, Td)
                                v.wait_ge(sG.h, GX)
                                if om:
                                    v.wait_ge(sG.h, om)
                                sV.inc(v.tensor_tensor(
                                    oc_b[it % 2][:, :], pd,
                                    xc_b[dhc % 2][:, 512 * half:512 * half + 512], op.add))
                            block.vector(radd)
                            d_marks[it] = sV.n
                            Vd = sV.n

                            def ost(g, dhc=dhc, half=half, it=it, Vd=Vd):
                                g.wait_ge(sV.h, Vd)
                                sG.inc(g.dma_start(
                                    out=x_dst[128 * dhc:128 * dhc + 128,
                                              co0 + 512 * half:co0 + 512 * half + 512],
                                    in_=oc_b[it % 2][:, :]), 16)
                            block.gpsimd(ost)
                            o_marks[it] = sG.n
                barrier()

        def phase_b():
            rope_marks = {}
            swap_marks = {}
            evq_marks = {}
            stq_marks = {}
            exp_done = {0: 0, 1: 0}
            pv_done = {0: 0, 1: 0}
            y_free = {0: 0, 1: 0}
            den_free = {0: 0, 1: 0}
            pj_marks = {}
            resb_marks = {}
            xuse_marks = {}

            scope = ExitStack()
            with scope:
                xn = scope.enter_context(nc.sbuf_tensor("b_xn", [128, KC * 2048], BF16))
                emit_ln(res, xn, [(512 * i, 512 * i) for i in range(NB)], 2048)
                barrier()

                wab_b = [scope.enter_context(
                    nc.sbuf_tensor(f"b_wab{i}", [128, KC * 128], BF16)) for i in range(2)]
                qn = scope.enter_context(nc.sbuf_tensor("b_qn", [128, 512], BF16))
                t1 = scope.enter_context(nc.sbuf_tensor("b_t1", [128, 512], F32))
                t2 = scope.enter_context(nc.sbuf_tensor("b_t2", [128, 512], F32))
                qr = scope.enter_context(nc.sbuf_tensor("b_qr", [128, 512], BF16))

                T0 = sT.n
                for mc in range(20):
                    def wabl(g, mc=mc, T0=T0):
                        if mc >= 2:
                            g.wait_ge(sT.h, T0 + (KC + 1) * NB * (mc - 1))
                        sG.inc(g.dma_start(
                            out=wab_b[mc % 2][:, :].rearrange("p (k c) -> p k c", c=128),
                            in_=wab_in[mc].rearrange("k p c -> p k c")), 16)
                    block.gpsimd(wabl)
                    GW = sG.n

                    for tb in range(NB):
                        it = mc * NB + tb
                        ph = pbt[0][:, 512 * (it % 2):512 * (it % 2) + 512]
                        psw = pbt[1][:, 512 * (it % 2):512 * (it % 2) + 512]
                        evm = evq_marks.get(it - 2, 0)

                        def mq(t, mc=mc, tb=tb, ph=ph, GW=GW, evm=evm):
                            t.wait_ge(sG.h, GW)
                            if evm:
                                t.wait_ge(sV.h, evm)
                            for kc in range(KC):
                                sT.inc(t.matmul(
                                    ph, wab_b[mc % 2][:, 128 * kc:128 * kc + 128],
                                    xn[:, 2048 * kc + 512 * tb:2048 * kc + 512 * tb + 512],
                                    start=(kc == 0), stop=(kc == KC - 1)))
                        block.tensor(mq)
                        Tq = sT.n
                        swm = swap_marks.get(it - 1, 0)
                        rpm = rope_marks.get(it - 1, 0)

                        def evq(v, ph=ph, Tq=Tq, swm=swm):
                            v.wait_ge(sT.h, Tq)
                            if swm:
                                v.wait_ge(sT.h, swm)
                            sV.inc(v.tensor_copy(qn[:, :], ph))
                        block.vector(evq)
                        evq_marks[it] = sV.n
                        Vq = sV.n
                        rp2 = rope_marks.get(it - 2, 0)

                        def msw(t, psw=psw, Vq=Vq, rp2=rp2):
                            t.wait_ge(sV.h, Vq)
                            if rp2:
                                t.wait_ge(sV.h, rp2)
                            sT.inc(t.matmul(psw, perm[:, :], qn[:, :],
                                            start=True, stop=True))
                        block.tensor(msw)
                        swap_marks[it] = sT.n
                        Tsw = sT.n
                        sqm = stq_marks.get(it - 1, 0)

                        def rope(v, tb=tb, psw=psw, Tsw=Tsw, sqm=sqm):
                            v.wait_ge(sT.h, Tsw)
                            if sqm:
                                v.wait_ge(sG.h, sqm)
                            sV.inc(v.tensor_tensor(
                                t1[:, :], qn[:, :],
                                cos_sb[:, 512 * tb:512 * tb + 512], op.mult))
                            sV.inc(v.tensor_tensor(
                                t2[:, :], psw,
                                sin_sb[:, 512 * tb:512 * tb + 512], op.mult))
                            sV.inc(v.tensor_tensor(qr[:, :], t1[:, :], t2[:, :], op.add))
                        block.vector(rope)
                        rope_marks[it] = sV.n
                        Vr = sV.n

                        def stq(g, mc=mc, tb=tb, Vr=Vr):
                            g.wait_ge(sV.h, Vr)
                            if mc < 16:
                                dst = qb_d[128 * mc:128 * mc + 128,
                                           512 * tb:512 * tb + 512]
                            else:
                                dst = kb_in[128 * (mc - 16):128 * (mc - 16) + 128,
                                            512 * tb:512 * tb + 512]
                            sG.inc(g.dma_start(out=dst, in_=qr[:, :]), 16)
                        block.gpsimd(stq)
                        stq_marks[it] = sG.n

                wvb = scope.enter_context(nc.sbuf_tensor("b_wv", [128, KC * 512], BF16))
                vev = scope.enter_context(nc.sbuf_tensor("b_vev", [128, 512], BF16))

                def wvlb(g):
                    if sT.n:
                        g.wait_ge(sT.h, sT.n)
                    sG.inc(g.dma_start(
                        out=wvb[:, :].rearrange("p (k c) -> p k c", c=512),
                        in_=wabv_in.rearrange("k p c -> p k c")), 16)
                block.gpsimd(wvlb)
                GV = sG.n
                for ts_ in range(16):
                    Vpre = sV.n

                    def mvv(t, ts_=ts_, GV=GV, Vpre=Vpre):
                        t.wait_ge(sG.h, GV)
                        if Vpre:
                            t.wait_ge(sV.h, Vpre)
                        for kc in range(KC):
                            sT.inc(t.matmul(
                                pbt[2][:, 0:512],
                                xn[:, 2048 * kc + 128 * ts_:2048 * kc + 128 * ts_ + 128],
                                wvb[:, 512 * kc:512 * kc + 512],
                                start=(kc == 0), stop=(kc == KC - 1)))
                    block.tensor(mvv)
                    Tv = sT.n
                    Gpre = sG.n

                    def evv2(v, Tv=Tv, Gpre=Gpre):
                        v.wait_ge(sT.h, Tv)
                        v.wait_ge(sG.h, Gpre)
                        sV.inc(v.tensor_copy(vev[:, :], pbt[2][:, 0:512]))
                    block.vector(evv2)
                    Vv = sV.n

                    def stv2(g, ts_=ts_, Vv=Vv):
                        g.wait_ge(sV.h, Vv)
                        sG.inc(g.dma_start(out=vb_in[128 * ts_:128 * ts_ + 128, :],
                                           in_=vev[:, :]), 16)
                    block.gpsimd(stv2)

                Gv = sG.n

                def ccb(g):
                    g.wait_ge(sG.h, Gv)
                    sC.inc(g.collective_compute(
                        "AllGather", mybir.AluOpType.bypass, replica_groups=PAIRS,
                        ins=[kb_in.ap().opt()], outs=[kb_g.ap().opt()]))
                    sC.inc(g.collective_compute(
                        "AllGather", mybir.AluOpType.bypass, replica_groups=PAIRS,
                        ins=[vb_in.ap().opt()], outs=[vb_g.ap().opt()]))
                block.gpsimd(ccb)
            barrier()

            scope = ExitStack()
            with scope:
                kbT = scope.enter_context(nc.sbuf_tensor("kbT", [128, 8 * 2048], BF16))
                vbs = scope.enter_context(nc.sbuf_tensor("vbs", [128, 32 * 512], BF16))
                msk = scope.enter_context(nc.sbuf_tensor("msk", [128, 8 * 512], BF16))
                qTb = scope.enter_context(nc.sbuf_tensor("qTb", [128, 16 * 512], BF16))
                yT = scope.enter_context(nc.sbuf_tensor("yT", [128, 16 * 512], BF16))
                ypj = scope.enter_context(nc.sbuf_tensor("ypj", [128, 16 * 512], F32))
                xbl2 = [scope.enter_context(nc.sbuf_tensor(f"b_x{i}", [128, 512], F32))
                        for i in range(2)]
                ob = [scope.enter_context(nc.sbuf_tensor(f"b_ob{i}", [128, 512], F32))
                      for i in range(2)]
                Eb = [scope.enter_context(nc.sbuf_tensor(f"Eb{i}", [128, 1024], BF16))
                      for i in range(2)]
                Em = scope.enter_context(nc.sbuf_tensor("Em", [128, 1024], BF16))
                sqb = scope.enter_context(nc.sbuf_tensor("sqb", [128, 512], BF16))
                t1b = scope.enter_context(nc.sbuf_tensor("t1b", [128, 512], F32))
                wp_b = [scope.enter_context(
                    nc.sbuf_tensor(f"b_wp{i}", [128, KC * 128], BF16)) for i in range(2)]

                C0 = sC.n

                def ldkvb(g):
                    g.wait_ge(sC.h, C0)
                    for ch in range(8):
                        sG.inc(g.dma_start(out=kbT[:, 2048 * ch:2048 * ch + 2048],
                                           in_=kb_g[128 * ch:128 * ch + 128, :]), 16)
                    for vc in range(32):
                        sG.inc(g.dma_start(out=vbs[:, 512 * vc:512 * vc + 512],
                                           in_=vb_g[128 * vc:128 * vc + 128, :]), 16)
                block.gpsimd(ldkvb)
                GKV = sG.n

                em_pv = [0]

                for qb in range(NB):
                    KT = KT_SCHED[qb]
                    Tpre = sT.n

                    def ldq2(sy, qb=qb, Tpre=Tpre):
                        if Tpre:
                            sy.wait_ge(sT.h, Tpre)
                        for c in range(KC):
                            sL.inc(sy.dma_start(
                                out=qTb[:, 512 * c:512 * c + 512],
                                in_=qb_d[128 * c:128 * c + 128, 512 * qb:512 * qb + 512]), 16)
                    block.sync(ldq2)
                    Vmsk = sV.n

                    def ldmsk(g, qb=qb, Vmsk=Vmsk):
                        if Vmsk:
                            g.wait_ge(sV.h, Vmsk)
                        sG.inc(g.dma_start(
                            out=msk[:, :].rearrange("p (t c) -> p t c", c=512),
                            in_=cmask_in[qb].rearrange("q p c -> p q c")), 16)
                    block.gpsimd(ldmsk)
                    GMSK = sG.n
                    LQ = sL.n

                    for h in range(NHB):
                        kh = h // 4
                        po = pbt[2 + (h % 2)][:, 0:512]
                        pden = pbt[2 + (h % 2)][0:1, 512:1024]
                        pbc = pbt[2 + (h % 2)][:, 512:1024]
                        ngrp = KT // 2

                        for g2 in range(ngrp):
                            ps = pbt[g2 % 2]
                            ed = exp_done[g2 % 2]

                            def mqk2(t, h=h, g2=g2, kh=kh, ps=ps, ed=ed, LQ=LQ):
                                t.wait_ge(sL.h, LQ)
                                t.wait_ge(sG.h, GKV)
                                if ed:
                                    t.wait_ge(sA.h, ed)
                                for half in range(2):
                                    kt = 2 * g2 + half
                                    p, slot, sub = kt_loc(kt)
                                    ch = 4 * p + kh
                                    col = 512 * slot + 128 * sub
                                    sT.inc(t.matmul(
                                        ps[:, 512 * half:512 * half + 512],
                                        kbT[:, 2048 * ch + col:2048 * ch + col + 128],
                                        qTb[:, 512 * h:512 * h + 512],
                                        start=True, stop=True))
                            block.tensor(mqk2)
                            Tqk = sT.n
                            pvd = pv_done[g2 % 2]

                            def ex2(s, g2=g2, ps=ps, Tqk=Tqk, pvd=pvd):
                                s.wait_ge(sT.h, Tqk)
                                if pvd:
                                    s.wait_ge(sT.h, pvd)
                                sA.inc(s.activation(Eb[g2 % 2][:, :], ps[:, :],
                                                    AF.Exp, scale=SC_B))
                            block.scalar(ex2)
                            exp_done[g2 % 2] = sA.n
                            Aex = sA.n

                            masked = (2 * g2 >= KT - 8)
                            if masked:
                                emv = em_pv[0]

                                def msks(v, g2=g2, Aex=Aex, KT=KT, emv=emv,
                                         GMSK=GMSK):
                                    v.wait_ge(sA.h, Aex)
                                    v.wait_ge(sG.h, GMSK)
                                    if emv:
                                        v.wait_ge(sT.h, emv)
                                    for half in range(2):
                                        kt = 2 * g2 + half
                                        mi = kt - (KT - 8)
                                        sV.inc(v.tensor_tensor(
                                            Em[:, 512 * half:512 * half + 512],
                                            Eb[g2 % 2][:, 512 * half:512 * half + 512],
                                            msk[:, 512 * mi:512 * mi + 512], op.mult))
                                block.vector(msks)
                                Vm = sV.n
                            else:
                                Vm = 0

                            yf = y_free[h % 2]
                            df = den_free[h % 2]

                            def mpv2(t, h=h, g2=g2, kh=kh, po=po, pden=pden,
                                     Aex=Aex, Vm=Vm, KT=KT, masked=masked,
                                     yf=yf, df=df):
                                if masked:
                                    t.wait_ge(sV.h, Vm)
                                else:
                                    t.wait_ge(sA.h, Aex)
                                if g2 == 0:
                                    if yf:
                                        t.wait_ge(sV.h, yf)
                                    if df:
                                        t.wait_ge(sA.h, df)
                                ebuf = Em if masked else Eb[g2 % 2]
                                for half in range(2):
                                    kt = 2 * g2 + half
                                    p, slot, sub = kt_loc(kt)
                                    vc = 16 * p + 4 * slot + sub
                                    sT.inc(t.matmul(
                                        po,
                                        vbs[:, 512 * vc + 128 * kh:512 * vc + 128 * kh + 128],
                                        ebuf[:, 512 * half:512 * half + 512],
                                        start=(kt == 0), stop=(kt == KT - 1)))
                                    sT.inc(t.matmul(
                                        pden, onesc[:, :],
                                        ebuf[:, 512 * half:512 * half + 512],
                                        start=(kt == 0), stop=(kt == KT - 1)))
                            block.tensor(mpv2)
                            pv_done[g2 % 2] = sT.n
                            if masked:
                                em_pv[0] = sT.n

                        Tpv = sT.n

                        def rc2(v, pden=pden, Tpv=Tpv):
                            v.wait_ge(sT.h, Tpv)
                            sV.inc(v.reciprocal(row_a[:, :], pden))
                        block.vector(rc2)
                        Vr = sV.n

                        def bc2(t, pbc=pbc, Vr=Vr):
                            t.wait_ge(sV.h, Vr)
                            sT.inc(t.matmul(pbc[:, 0:512], onesrf[:, :], row_a[:, :],
                                            start=True, stop=True))
                        block.tensor(bc2)
                        Tb2 = sT.n

                        def evb2(s, pbc=pbc, Tb2=Tb2):
                            s.wait_ge(sT.h, Tb2)
                            sA.inc(s.activation(bc_a[:, :], pbc[:, 0:512], AF.Copy))
                        block.scalar(evb2)
                        den_free[h % 2] = sA.n
                        Ab = sA.n

                        def nm2(v, h=h, po=po, Ab=Ab):
                            v.wait_ge(sA.h, Ab)
                            sV.inc(v.tensor_tensor(yT[:, 512 * h:512 * h + 512],
                                                   po, bc_a[:, :], op.mult))
                        block.vector(nm2)
                        y_free[h % 2] = sV.n

                    YV = sV.n

                    T1 = sT.n
                    for mc in range(KC):
                        def wpl(g, mc=mc, qb=qb, T1=T1):
                            if mc >= 2:
                                g.wait_ge(sT.h, T1 + (KC + 1) * (mc - 1))
                            sG.inc(g.dma_start(
                                out=wp_b[mc % 2][:, :].rearrange("p (k c) -> p k c", c=128),
                                in_=wp_in[mc].rearrange("k p c -> p k c")), 16)
                        block.gpsimd(wpl)
                        GP = sG.n
                        pp = pbt[0][:, 512 * (mc % 2):512 * (mc % 2) + 512]
                        pjm = pj_marks.get((qb, mc - 2), 0)

                        def mpj(t, mc=mc, pp=pp, GP=GP, YV=YV, pjm=pjm):
                            t.wait_ge(sG.h, GP)
                            t.wait_ge(sV.h, YV)
                            if pjm:
                                t.wait_ge(sV.h, pjm)
                            for kc in range(KC):
                                sT.inc(t.matmul(
                                    pp, wp_b[mc % 2][:, 128 * kc:128 * kc + 128],
                                    yT[:, 512 * kc:512 * kc + 512],
                                    start=(kc == 0), stop=(kc == KC - 1)))
                        block.tensor(mpj)
                        Tp = sT.n

                        def evp(v, mc=mc, pp=pp, Tp=Tp):
                            v.wait_ge(sT.h, Tp)
                            sV.inc(v.tensor_copy(ypj[:, 512 * mc:512 * mc + 512], pp))
                        block.vector(evp)
                        pj_marks[(qb, mc)] = sV.n
                        Vp = sV.n

                        def sqp(s, mc=mc, Vp=Vp):
                            s.wait_ge(sV.h, Vp)
                            sA.inc(s.activation(sqb[:, :],
                                                ypj[:, 512 * mc:512 * mc + 512],
                                                AF.Square))
                        block.scalar(sqp)
                        Ap = sA.n

                        def mst(t, mc=mc, Ap=Ap):
                            t.wait_ge(sA.h, Ap)
                            sT.inc(t.matmul(pbt[1][0:1, 0:512], onesc[:, :], sqb[:, :],
                                            start=(mc == 0), stop=(mc == KC - 1)))
                        block.tensor(mst)

                    Ts = sT.n

                    def rmsrow(v, Ts=Ts):
                        v.wait_ge(sT.h, Ts)
                        sV.inc(v.tensor_scalar(row_a[:, :], pbt[1][0:1, 0:512],
                                               1.0 / D, 1e-6, op.mult, op.add))
                    block.vector(rmsrow)
                    Vv2 = sV.n

                    def rstd2(s, Vv2=Vv2):
                        s.wait_ge(sV.h, Vv2)
                        sA.inc(s.activation(row_b[:, :], row_a[:, :], AF.Ln))
                        sA.inc(s.activation(row_a[:, :], row_b[:, :], AF.Exp, scale=-0.5))
                    block.scalar(rstd2)
                    Av3 = sA.n

                    def bcr(t, Av3=Av3):
                        t.wait_ge(sA.h, Av3)
                        sT.inc(t.matmul(pbt[1][:, 512:1024], onesrf[:, :], row_a[:, :],
                                        start=True, stop=True))
                    block.tensor(bcr)
                    Tb3 = sT.n

                    def evr(s, Tb3=Tb3):
                        s.wait_ge(sT.h, Tb3)
                        sA.inc(s.activation(bc_b[:, :], pbt[1][:, 512:1024], AF.Copy))
                    block.scalar(evr)
                    Ar = sA.n

                    for mc in range(KC):
                        rsm = resb_marks.get((qb, mc - 2), 0)
                        xum = xuse_marks.get((qb, mc - 2), 0)

                        def lx2(g, mc=mc, qb=qb, rsm=rsm, xum=xum):
                            if rsm:
                                g.wait_ge(sG.h, rsm)
                            if xum:
                                g.wait_ge(sV.h, xum)
                            sG.inc(g.dma_start(
                                out=xbl2[mc % 2][:, :],
                                in_=res[128 * mc:128 * mc + 128,
                                        512 * qb:512 * qb + 512]), 16)
                        block.gpsimd(lx2)
                        Gx = sG.n
                        obm = resb_marks.get((qb, mc - 2), 0)

                        def fin(v, mc=mc, Ar=Ar, Gx=Gx, obm=obm):
                            v.wait_ge(sA.h, Ar)
                            v.wait_ge(sG.h, Gx)
                            if obm:
                                v.wait_ge(sG.h, obm)
                            sV.inc(v.tensor_tensor(t1b[:, :],
                                                   ypj[:, 512 * mc:512 * mc + 512],
                                                   bc_b[:, :], op.mult))
                            sV.inc(v.scalar_tensor_tensor(
                                ob[mc % 2][:, :], t1b[:, :], nw_sb[:, mc:mc + 1],
                                xbl2[mc % 2][:, :], op.mult, op.add))
                        block.vector(fin)
                        xuse_marks[(qb, mc)] = sV.n
                        Vf = sV.n

                        def stf(g, mc=mc, qb=qb, Vf=Vf):
                            g.wait_ge(sV.h, Vf)
                            sG.inc(g.dma_start(
                                out=res[128 * mc:128 * mc + 128,
                                        512 * qb:512 * qb + 512],
                                in_=ob[mc % 2][:, :]), 16)
                        block.gpsimd(stf)
                        resb_marks[(qb, mc)] = sG.n

        if CUT >= 1:
            phase_a1()
            barrier()
        if CUT >= 2:
            phase_a2()
            barrier()
        if CUT >= 3:
            phase_ffn(res, res, 0)
        if CUT >= 4:
            phase_b()
            barrier()
        if CUT >= 5:
            phase_ffn(res, outT, 1)
        else:
            src = res.ap() if CUT >= 2 else xT

            def dump(g):
                g.wait_ge(sG.h, sG.n)
                g.wait_ge(sV.h, sV.n)
                sG.inc(g.dma_start(out=outT, in_=src), 16)
            block.gpsimd(dump)

        def final(g):
            g.wait_ge(sG.h, sG.n)
            g.wait_ge(sL.h, sL.n)
            g.wait_ge(sT.h, sT.n)
            g.wait_ge(sA.h, sA.n)
            g.wait_ge(sV.h, sV.n)
            g.wait_ge(sC.h, sC.n)
        block.gpsimd(final)

    return nc


def kernel(**inputs):
    x = np.ascontiguousarray(np.asarray(inputs["L0_attn_a_x"], dtype=np.float32))
    Wq = np.asarray(inputs["L0_attn_a_Wq"], dtype=np.float32)
    Wk = np.asarray(inputs["L0_attn_a_Wk"], dtype=np.float32)
    Wv = np.asarray(inputs["L0_attn_a_Wv"], dtype=np.float32)
    memory = np.asarray(inputs["L0_attn_a_memory"], dtype=np.float32)
    memory_norm = np.asarray(inputs["L0_attn_a_memory_norm"], dtype=np.float32)
    memory_weight = np.asarray(inputs["L0_attn_a_memory_weight"], dtype=np.float32)
    w_attn = np.asarray(inputs["L0_attn_b_w_attn"], dtype=np.float32)
    w_proj = np.asarray(inputs["L0_attn_b_w_proj"], dtype=np.float32)
    norm_w = np.asarray(inputs["L0_attn_b_norm_w"], dtype=np.float32)
    lut = np.asarray(inputs["L0_ffn_lut"], dtype=np.float32)
    gate_idx = np.asarray(inputs["L0_ffn_gate_idx"], dtype=np.int32)
    up_idx = np.asarray(inputs["L0_ffn_up_idx"], dtype=np.int32)
    down_idx = np.asarray(inputs["L0_ffn_down_idx"], dtype=np.int32)

    def blk(a, r, c):
        R, C = a.shape
        return np.ascontiguousarray(
            a.reshape(R // r, r, C // c, c).transpose(2, 0, 1, 3))

    wq_blk = blk(np.ascontiguousarray(Wq.T), 128, 128)
    wk_blk = blk(np.ascontiguousarray(Wk.T), 128, 128)
    wv_blk = np.ascontiguousarray(Wv.T.reshape(16, 128, 512))
    mem_blk = np.ascontiguousarray(
        memory.reshape(4, 128, 4, 128).transpose(0, 2, 1, 3))
    mnorm_blk = np.ascontiguousarray(memory_norm.reshape(4, 128).T)
    mw = np.ascontiguousarray(memory_weight.reshape(1, 1))
    wab_blk = blk(np.ascontiguousarray(w_attn[:, 0:2560]), 128, 128)
    wabv_blk = np.ascontiguousarray(w_attn[:, 2560:3072].reshape(16, 128, 512))
    wp_blk = blk(np.ascontiguousarray(w_proj), 128, 128)
    nw = np.ascontiguousarray(norm_w.reshape(16, 128).T)
    lut2 = np.ascontiguousarray(lut.reshape(1, 16))
    idxgT = np.ascontiguousarray(gate_idx.T)
    idxuT = np.ascontiguousarray(up_idx.T)
    idxdT = np.zeros((6144, 2048), np.int32)
    idxdT[0:5504] = down_idx.T
    ident = np.eye(128, dtype=np.float32)
    permm = np.zeros((128, 128), np.float32)
    for i in range(128):
        permm[i ^ 1, i] = 1.0

    hd = 128
    invf = 1.0 / 10000.0 ** (np.arange(0, hd, 2, dtype=np.float64) / hd)

    in_maps = []
    shard_tokens = []
    for r in range(8):
        b = r // 2
        blocks = G_LO if r % 2 == 0 else G_HI
        toks = np.concatenate([np.arange(512 * g, 512 * g + 512) for g in blocks])
        shard_tokens.append((b, toks))
        xTc = np.ascontiguousarray(x[b][toks].T)
        ang = toks.astype(np.float64)[None, :] * invf[:, None]
        cosb = np.zeros((128, TOK), np.float32)
        sinb = np.zeros((128, TOK), np.float32)
        cosb[0::2] = np.cos(ang).astype(np.float32)
        cosb[1::2] = np.cos(ang).astype(np.float32)
        sinb[0::2] = (-np.sin(ang)).astype(np.float32)
        sinb[1::2] = np.sin(ang).astype(np.float32)
        cm = np.zeros((4, 8, 128, 512), np.float32)
        for s in range(4):
            g = blocks[s]
            KT = KT_SCHED[s]
            qpos = np.arange(512 * g, 512 * g + 512)
            for p in range(8):
                kt = KT - 8 + p
                kpos = np.arange(128 * kt, 128 * kt + 128)
                cm[s, p] = (kpos[:, None] <= qpos[None, :]).astype(np.float32)
        in_maps.append({
            "xT": xTc,
            "wq_blk": wq_blk, "wk_blk": wk_blk, "wv_blk": wv_blk,
            "mem_blk": mem_blk, "mnorm_blk": mnorm_blk, "mw": mw,
            "wab_blk": wab_blk, "wabv_blk": wabv_blk, "wp_blk": wp_blk,
            "nw": nw,
            "idxg_sh": np.ascontiguousarray(idxgT[256 * r:256 * r + 256]),
            "idxu_sh": np.ascontiguousarray(idxuT[256 * r:256 * r + 256]),
            "idxd_sh": np.ascontiguousarray(idxdT[768 * r:768 * r + 768]),
            "lut": lut2, "cosb": cosb, "sinb": sinb, "cmask": cm,
            "ident": ident, "perm": permm,
        })

    nc = build_program()
    resr = run_bass_kernel_spmd(nc, in_maps, core_ids=list(range(8)))

    out = np.empty((4, S_FULL, D), np.float32)
    for r in range(8):
        b, toks = shard_tokens[r]
        out[b][toks] = resr.results[r]["outT"].T
    return out
